# version 09 (final)
# baseline (speedup 1.0000x reference)
import os
from contextlib import ExitStack

import numpy as np
import ml_dtypes

import concourse.bass as bass
import concourse.mybir as mybir
from concourse.bass_utils import run_bass_kernel_spmd

F32 = mybir.dt.float32
BF16 = mybir.dt.bfloat16
I32 = mybir.dt.int32
U32 = mybir.dt.uint32
AF = mybir.ActivationFunctionType
ALU = mybir.AluOpType
AX = mybir.AxisListType
BF16NP = ml_dtypes.bfloat16

NCORES = 8
TB = 8192
DM = 1024
NH = 16
STRIP = 512
NSLOT = 4
LHAT = [64, 48, 32, 16]
GB = [n // 2 for n in LHAT]
Z = 8192
HID = 3072
EPS = 1e-5
L0_BIAS = -10.0
L1_BIAS = -32.0
QSCALE = 0.125
FS = float(np.float32(1.0 / 127.0))
FS2 = float(np.float32(1.0 / 127.0) * np.float32(1.0 / 127.0))
INV_D = float(np.float32(1.0 / DM))
INV_NG = float(np.float32(1.0 / (TB * DM)))
TOK = 2048
KCOLS = Z + TB
V65 = NH * 65
CONST_VALS = [L0_BIAS, L1_BIAS, QSCALE, 1.0, 0.0, EPS, FS, FS2, INV_D, INV_NG]


def slots_for_core(c):
    cc = c % 4
    return [15 - cc, 8 + cc, 7 - cc, cc]


def core_token_index(c):
    idx = []
    for s in slots_for_core(c):
        idx.extend(range(s * STRIP, (s + 1) * STRIP))
    return np.array(idx)


def diag_mask_np(g, t):
    p = np.arange(128)[:, None]
    f = np.arange(512)[None, :]
    thresh = 512 - 256 * (g + 1) + 128 * t
    return (p <= f - thresh).astype(BF16NP)


def mask_tile_np(g):
    p0 = diag_mask_np(g, 0)
    p1 = diag_mask_np(g, 1)
    return np.concatenate([p0, p0, p1, p1], axis=1)


ENGS = ("sy", "ge", "te", "sc", "ve")
DMA_SEM = {"sy": "dSY", "ge": "dGE", "sc": "dSC"}
ENG_SEM = {"te": "PE", "sc": "ACT", "ve": "DVE", "ge": "PL"}
ALL_SEMS = ("dSY", "dGE", "dSC", "PE", "ACT", "DVE", "PL", "CC", "FIN")


class Tok:
    __slots__ = ("sem", "val")

    def __init__(self, sem, val):
        self.sem = sem
        self.val = val


class KB:
    def __init__(self, nc):
        self.nc = nc
        self.streams = {e: [] for e in ENGS}
        self.counts = {s: 0 for s in ALL_SEMS}
        self.sems = {s: nc.semaphore(s).__enter__() for s in ALL_SEMS}
        self.es = ExitStack()
        self.scratch_stack = []
        self.guard = {}
        owned = {"sy": ["dSY"], "ge": ["dGE", "PL", "CC", "FIN"],
                 "sc": ["dSC", "ACT"], "te": ["PE"], "ve": ["DVE"]}
        for e in ENGS:
            for sname in owned[e]:
                self.emit(e, lambda en, sname=sname: en.sem_clear(self.sems[sname]))
            for _ in range(24):
                self.emit(e, lambda en: en.nop(cycle_cnt=12000))
        for e in ENGS:
            self.emit(e, lambda en: en.nop(), inc="FIN", inc_n=1)
        start_tok = Tok("FIN", self.counts["FIN"])
        for e in ENGS:
            self.emit(e, None, waits=[start_tok])

    def sb(self, name, shape, dtype, scratch=False):
        es = self.scratch_stack[-1] if scratch else self.es
        return es.enter_context(self.nc.sbuf_tensor(name, shape, dtype))

    def begin_scratch(self):
        self.scratch_stack.append(ExitStack())

    def end_scratch(self):
        self.scratch_stack.pop().close()

    def rd(self, key, tok):
        if tok is None:
            return
        d = self.guard.setdefault(key, {})
        d[tok.sem] = max(d.get(tok.sem, 0), tok.val)

    def wg(self, key):
        d = self.guard.pop(key, {})
        return [Tok(s, v) for s, v in d.items()]

    def emit(self, eng, fn, waits=(), inc=None, inc_n=1):
        ws = []
        for t in waits:
            if t is None:
                continue
            assert t.val <= self.counts[t.sem], (t.sem, t.val, self.counts[t.sem])
            ws.append(t)
        tok = None
        if inc is not None:
            self.counts[inc] += inc_n
            tok = Tok(inc, self.counts[inc])
        self.streams[eng].append((tuple(ws), fn, inc, inc_n,
                                  tok.val if tok else 0))
        return tok

    def dma(self, eng, out, in_, waits=(), **kw):
        return self.emit(eng, lambda e: e.dma_start(out=out, in_=in_, **kw),
                         waits, inc=DMA_SEM[eng], inc_n=16)

    def op(self, eng, fn, waits=(), inc=True):
        return self.emit(eng, fn, waits, inc=ENG_SEM[eng] if inc else None)

    def barrier(self):
        toks = [Tok(s, self.counts[s]) for s in ALL_SEMS if self.counts[s] > 0]
        for e in ENGS:
            self.emit(e, None, waits=toks)
        return toks

    def finalize(self):
        final_toks = [Tok(s, self.counts[s]) for s in ALL_SEMS if self.counts[s] > 0]
        for e in ENGS:
            self.emit(e, None, waits=final_toks)
        if os.environ.get("KDBG_XTAP"):
            for nm, xa in getattr(self, "_dbg_internal", {}).items():
                if not nm.startswith("x_after"):
                    continue
                dbg = self.nc.dram_tensor(f"dbg_{nm}", [DM, TOK], F32,
                                          kind="ExternalOutput")
                t = None
                for c8 in range(8):
                    t = self.dma("sc", bass.AP(dbg, c8 * 128 * TOK, [[TOK, 128], [1, TOK]]),
                                 bass.AP(xa, c8 * 128 * TOK, [[TOK, 128], [1, TOK]]),
                                 waits=final_toks)
                self.emit("sc", None, waits=[t])
        nc = self.nc
        with nc.Block() as block:
            for eng_name, bfn in (("sy", block.sync), ("ge", block.gpsimd),
                                  ("te", block.tensor), ("sc", block.scalar),
                                  ("ve", block.vector)):
                bfn(self._replayer(eng_name))
        self.es.close()

    def _replayer(self, eng_name):
        stream = self.streams[eng_name]
        sems = self.sems
        final_dma = self.counts.get(DMA_SEM.get(eng_name, ""), 0)

        def run(e):
            seen = {}
            for waits, fn, inc, inc_n, _ in stream:
                for t in waits:
                    if seen.get(t.sem, 0) >= t.val:
                        continue
                    e.wait_ge(sems[t.sem], t.val)
                    seen[t.sem] = t.val
                if fn is None:
                    continue
                ins = fn(e)
                if inc is not None:
                    ins.then_inc(sems[inc], inc_n)
        return run


class Build:
    def __init__(self, phases):
        self.phases = phases
        nc = bass.Bass("TRN2", target_bir_lowering=False)
        self.nc = nc
        kb = KB(nc)
        self.kb = kb
        self.inputs = {}
        self.internal = {}

        def inp(name, shape, dtype):
            t = nc.dram_tensor(name, shape, dtype, kind="ExternalInput")
            self.inputs[name] = t
            return t

        def itn(name, shape, dtype):
            t = nc.dram_tensor(name, shape, dtype, kind="Internal")
            self.internal[name] = t
            return t

        self.xin = inp("xin", [DM, TOK], F32)
        inp("consts", [128, len(CONST_VALS)], F32)
        inp("meta", [1, 16], U32)
        if "l0attn" in phases:
            inp("xb", [DM, TB], F32)
            inp("wqkvT", [DM, 3 * DM], F32)
            inp("woutT", [DM, DM], F32)
            inp("mask0", [128, 2048], BF16)
            inp("mask1", [128, 2048], BF16)
            itn("k_dram", [DM, KCOLS], BF16)
            itn("v_dram", [KCOLS, V65], BF16)
            for k in range(NSLOT):
                span = LHAT[k] * 128
                itn(f"walk_k{k}", [DM, span], BF16)
                itn(f"walk_v{k}", [span, V65], BF16)
        if "l0ffn" in phases:
            inp("l0gT", [DM, HID], I32)
            inp("l0uT", [DM, HID], I32)
            inp("l0dT", [HID, DM], I32)
        if "l1attn" in phases:
            inp("l1wT", [DM, 3 * DM], F32)
            itn("y_dram", [DM, TOK], F32)
            itn("cc_in", [1, 2], F32)
            itn("cc_out", [1, 2], F32)
        if "l1ffn" in phases:
            inp("l1gT", [DM, HID], I32)
            inp("l1uT", [DM, HID], I32)
            inp("l1dT", [HID, DM], I32)
        self.out = nc.dram_tensor("out", [DM, TOK], F32, kind="ExternalOutput")
        self.xdst_of = {}
        for i, p in enumerate(phases):
            self.xdst_of[p] = (self.out if i == len(phases) - 1
                               else itn(f"x_after_{p}", [DM, TOK], F32))

        self.psa = kb.es.enter_context(nc.psum_tensor("psa", [128, 2048], F32))
        self.pb = [kb.es.enter_context(nc.psum_tensor(f"pb{i}", [128, 512], F32))
                   for i in range(4)]

        self.cpool = kb.sb("cpool", [128, len(CONST_VALS)], F32)
        for i, v in enumerate(CONST_VALS):
            nc.const_aps.register(F32, float(v), self.cpool[:, i:i + 1])
        self.meta_sb = kb.sb("meta_sb", [1, 16], U32)
        self.ones128 = kb.sb("ones128", [128, 1], F32)
        self.ones1 = kb.sb("ones1", [1, 128], F32)
        self.srow = {n: kb.sb(f"srow_{n}", [1, 512], F32)
                     for n in ("m", "q", "r", "t", "u", "c", "mc", "rr")}
        self.gstat = kb.sb("gstat", [1, 16], F32)
        self.zero_sb = kb.sb("zero_sb", [128, 1040], BF16)

        kb.dma("sy", self.cpool[:, :], self.inputs["consts"].ap())
        kb.dma("sy", self.meta_sb[:, :], self.inputs["meta"].ap())
        kb.op("ve", lambda e: e.memset(self.ones128[:, :], 1.0))
        kb.op("ve", lambda e: e.memset(self.ones1[:, :], 1.0))
        kb.op("ve", lambda e: e.memset(self.zero_sb[:, :], 0.0))
        kb.op("ve", lambda e: e.memset(self.gstat[:, :], 0.0))
        if "l0attn" in phases:
            self.mask0 = kb.sb("mask0_sb", [128, 2048], BF16)
            self.mask1 = kb.sb("mask1_sb", [128, 2048], BF16)
            kb.dma("sy", self.mask0[:, :], self.inputs["mask0"].ap())
            kb.dma("sy", self.mask1[:, :], self.inputs["mask1"].ap())
        self.init_toks = [Tok("dSY", kb.counts["dSY"]), Tok("DVE", kb.counts["DVE"])]
        for e in ENGS:
            kb.emit(e, None, waits=self.init_toks)

    def mm(self, ps, lhsT, rhs, start, stop, waits=(), inc=False):
        return self.kb.op("te", lambda e, ps=ps, lhsT=lhsT, rhs=rhs, a=start, b=stop:
                          e.matmul(ps, lhsT, rhs, start=a, stop=b,
                                   skip_group_check=True),
                          waits=waits, inc=inc)

    def ln_stats(self, tiles, sq_t, w=512):
        kb = self.kb
        n = len(tiles)
        sq_toks = []
        for i, (xa, xt) in enumerate(tiles):
            sqa = sq_t[:, i * w:(i + 1) * w]
            t = kb.op("ge", lambda e, xa=xa, sqa=sqa:
                      e.tensor_tensor(sqa, xa, xa, ALU.mult),
                      waits=[xt] + kb.wg(("sq", i)))
            sq_toks.append(t)
        s_tok = None
        for i, (xa, xt) in enumerate(tiles):
            s_tok = self.mm(self.pb[3][0:1, 0:w], self.ones128[:, :], xa,
                            i == 0, i == n - 1,
                            waits=[xt] + (kb.wg("pb3") if i == 0 else []),
                            inc=(i == n - 1))
        q_tok = None
        for i in range(n):
            sqa = sq_t[:, i * w:(i + 1) * w]
            q_tok = self.mm(self.pb[1][0:1, 0:w], self.ones128[:, :], sqa,
                            i == 0, i == n - 1,
                            waits=[sq_toks[i]] + (kb.wg("pb1") if i == 0 else []),
                            inc=(i == n - 1))
        for i, t in enumerate(sq_toks):
            kb.rd(("sq", i), q_tok)
        m, q, r, t = (self.srow[k][0:1, 0:w] for k in ("m", "q", "r", "t"))
        tm = kb.op("ve", lambda e: e.tensor_scalar(m, self.pb[3][0:1, 0:w],
                                                   INV_D, None, ALU.mult),
                   waits=[s_tok] + kb.wg("srow_m"))
        tq = kb.op("ve", lambda e: e.tensor_scalar(q, self.pb[1][0:1, 0:w],
                                                   INV_D, None, ALU.mult),
                   waits=[q_tok] + kb.wg("srow_q"))
        kb.rd("pb3", tm)
        kb.rd("pb1", tq)
        t1 = kb.op("ve", lambda e: e.tensor_tensor(t, m, m, ALU.mult),
                   waits=[tm] + kb.wg("srow_t"))
        t2 = kb.op("ve", lambda e: e.tensor_tensor(t, q, t, ALU.subtract),
                   waits=[t1, tq])
        t3 = kb.op("sc", lambda e: e.activation(t, t, AF.Sqrt, bias=EPS),
                   waits=[t2])
        t4 = kb.op("ve", lambda e: e.reciprocal(r, t), waits=[t3] + kb.wg("srow_r"))
        kb.rd("srow_t", t4)
        return m, r, t4, sq_toks[-1]

    def bcast_pair(self, a_ap, b_ap, tok, w=512):
        kb = self.kb
        tA = self.mm(self.pb[2][:, 0:w], self.ones1[:, :], a_ap, True, True,
                     waits=[tok] + kb.wg("pb2"), inc=True)
        tB = self.mm(self.pb[1][:, 0:w], self.ones1[:, :], b_ap, True, True,
                     waits=[tok] + kb.wg("pb1"), inc=True)
        kb.rd("srow_m", tA)
        kb.rd("srow_q", tA)
        kb.rd("srow_r", tB)
        kb.rd("srow_c", tB)
        kb.rd("srow_mc", tB)
        return self.pb[2][:, 0:w], self.pb[1][:, 0:w], tA, tB

    def ln_apply(self, tiles, A, B, tA, tB, out_fn, tmp_f):
        kb = self.kb
        toks = []
        lastA = lastB = None
        for i, (xa, xt) in enumerate(tiles):
            t1 = kb.op("ve", lambda e, xa=xa: e.tensor_tensor(
                tmp_f[:, :], xa, A, ALU.subtract), waits=[tA, xt])
            t2 = kb.op("ve", lambda e, i=i, oa=out_fn(i): e.tensor_tensor(
                oa, tmp_f[:, :], B, ALU.mult), waits=[t1, tB])
            toks.append(t2)
            lastA, lastB = t1, t2
        kb.rd("pb2", lastA)
        kb.rd("pb1", lastB)
        return toks

    def emit_l0attn(self):
        kb, nc = self.kb, self.nc
        kd = self.internal["k_dram"]
        vd = self.internal["v_dram"]
        kb.begin_scratch()
        self.q_own = kb.sb("q_own", [128, 8 * TOK], BF16, scratch=True)
        self.attnout = kb.sb("attnout", [128, 8 * TOK], BF16, scratch=True)
        kb.begin_scratch()
        wq_sb = kb.sb("wq_sb", [128, 8 * 3 * DM], BF16, scratch=True)
        xb_t = [kb.sb(f"xb_t{i}", [128, 4096], F32, scratch=True) for i in range(2)]
        sq_t = kb.sb("sq_t", [128, 4096], F32, scratch=True)
        xln_t = kb.sb("xln_t", [128, 4096], BF16, scratch=True)
        tmp_f = kb.sb("tmp_f", [128, 512], F32, scratch=True)
        ksb = [kb.sb(f"ksb{i}", [128, 512], BF16, scratch=True) for i in range(2)]
        vsb = [kb.sb(f"vsb{i}", [128, 520], BF16, scratch=True) for i in range(2)]

        xb = self.inputs["xb"]
        zt = None
        for c in range(8):
            for j in range(8):
                zt = kb.dma("sc", bass.AP(kd, c * 128 * KCOLS + j * 1024,
                                          [[KCOLS, 128], [1, 1024]]),
                            self.zero_sb[:, 0:1024])
        for j in range(Z // 128):
            zt = kb.dma("sc", bass.AP(vd, j * 128 * V65, [[V65, 128], [1, V65]]),
                        self.zero_sb[:, 0:V65])
        zone_tok = zt

        wq = self.inputs["wqkvT"]
        wq_tok = None
        for dc in range(8):
            wq_tok = kb.dma("ge", wq_sb[:, dc * 3 * DM:(dc + 1) * 3 * DM],
                            bass.AP(wq, dc * 128 * 3 * DM, [[3 * DM, 128], [1, 3 * DM]]))

        vones = [kb.op("ve", lambda e, v=v: e.memset(v[:, :], 1.0)) for v in vsb]
        ev = {i: None for i in range(4)}
        last_kv_store = None
        last_q_cp = None

        def ln_tile(stride, src, off, buf, bufkey):
            dts = []
            for c in range(8):
                dts.append(kb.dma("sy", buf[:, c * 512:(c + 1) * 512],
                                  bass.AP(src, off + c * 128 * stride,
                                          [[stride, 128], [1, 512]]),
                                  waits=kb.wg((bufkey, c))))
            tiles = [(buf[:, c * 512:(c + 1) * 512], dts[c]) for c in range(8)]
            m, r, tmr, _ = self.ln_stats(tiles, sq_t)
            A, B, tA, tB = self.bcast_pair(m, r, tmr)
            xts = self.ln_apply(tiles, A, B, tA, tB,
                                lambda i: xln_t[:, i * 512:(i + 1) * 512], tmp_f)
            for c in range(8):
                kb.rd((bufkey, c), xts[c])
            return xts

        for ti in range(16):
            bk = ti % 2
            xts = ln_tile(TB, xb, ti * 512, xb_t[bk], f"xb{bk}")
            xln_tok = xts[-1]
            for ei in range(8):
                e0 = (8 + ei) * 128
                bank = ei % 4
                ps = self.psa[:, bank * 512:bank * 512 + 512]
                mt = None
                for dc in range(8):
                    lhsT = wq_sb[:, dc * 3 * DM + e0:dc * 3 * DM + e0 + 128]
                    rhs = xln_t[:, dc * 512:(dc + 1) * 512]
                    w = ([wq_tok, xln_tok, ev[bank]] if dc == 0 else [])
                    mt = self.mm(ps, lhsT, rhs, dc == 0, dc == 7, w, inc=(dc == 7))
                ib = ei % 2
                cp = kb.op("ve", lambda e, ps=ps, ib=ib: e.tensor_copy(ksb[ib][:, :], ps),
                           waits=[mt] + kb.wg(("ksb", ib)))
                ev[bank] = cp
                st = kb.dma("sc", bass.AP(kd, (ei * 128) * KCOLS + Z + ti * 512,
                                          [[KCOLS, 128], [1, 512]]),
                            ksb[ib][:, :], waits=[cp])
                kb.rd(("ksb", ib), st)
                last_kv_store = st
            for ts_ in range(4):
                for half in range(2):
                    bank = (ts_ * 2 + half) % 4
                    ps = self.psa[:, bank * 512:bank * 512 + 512]
                    mt = None
                    for dc in range(8):
                        lhsT = xln_t[:, dc * 512 + ts_ * 128:dc * 512 + ts_ * 128 + 128]
                        rhs = wq_sb[:, dc * 3 * DM + 2048 + half * 512:
                                    dc * 3 * DM + 2048 + half * 512 + 512]
                        w = ([xln_tok, ev[bank]] if dc == 0 else [])
                        mt = self.mm(ps, lhsT, rhs, dc == 0, dc == 7, w, inc=(dc == 7))
                    ib = (ts_ * 2 + half) % 2
                    out_ap = bass.AP(vsb[ib], 0, [[520, 128], [65, 8], [1, 64]])
                    in_ap = bass.AP(ps.tensor, ps.offset, [[2048, 128], [64, 8], [1, 64]])
                    cp = kb.op("ve", lambda e, o=out_ap, i_=in_ap: e.tensor_copy(o, i_),
                               waits=[mt, vones[ib]] + kb.wg(("vsb", ib)))
                    ev[bank] = cp
                    st = kb.dma("sc",
                                bass.AP(vd, (Z + ti * 512 + ts_ * 128) * V65 + half * 520,
                                        [[V65, 128], [1, 520]]),
                                vsb[ib][:, :], waits=[cp])
                    kb.rd(("vsb", ib), st)
                    last_kv_store = st
            if os.environ.get("KPAR2", "0") == "1":
                kb.barrier()

        for ti in range(4):
            bk = ti % 2
            xts = ln_tile(TOK, self.xin, ti * 512, xb_t[bk], f"xb{bk}")
            xln_tok = xts[-1]
            for ei in range(8):
                e0 = ei * 128
                bank = ei % 4
                ps = self.psa[:, bank * 512:bank * 512 + 512]
                mt = None
                for dc in range(8):
                    lhsT = wq_sb[:, dc * 3 * DM + e0:dc * 3 * DM + e0 + 128]
                    rhs = xln_t[:, dc * 512:(dc + 1) * 512]
                    w = ([wq_tok, xln_tok, ev[bank]] if dc == 0 else [])
                    mt = self.mm(ps, lhsT, rhs, dc == 0, dc == 7, w, inc=(dc == 7))
                cp = kb.op("ve", lambda e, ps=ps, ei=ei, ti=ti: e.tensor_copy(
                    self.q_own[:, ei * TOK + ti * 512:ei * TOK + ti * 512 + 512], ps),
                    waits=[mt])
                ev[bank] = cp
                last_q_cp = cp

        kb.end_scratch()
        kb.begin_scratch()
        kbuf = [kb.sb(f"kbuf{i}", [128, 256], BF16, scratch=True) for i in range(3)]
        vbuf = [kb.sb(f"vbuf{i}", [128, 260], BF16, scratch=True) for i in range(3)]
        pbuf = [kb.sb(f"pbuf{i}", [128, 2048], BF16, scratch=True) for i in range(2)]
        osb = kb.sb("osb", [64, 512], F32, scratch=True)

        start_tok = kb.op("te", lambda e: e.nop(),
                          waits=[last_q_cp, ev[0], ev[1], ev[2], ev[3]], inc=True)
        kfree = [start_tok] * 3
        vfree = [start_tok] * 3
        pfree = [start_tok] * 2
        psa_free = start_tok
        o_guard = {0: kb.wg("pb0"), 1: kb.wg("pb1")}
        stage_tok = {}
        r_sc = self.nc.scalar.alloc_register("stage_rk")
        r_ge = self.nc.gpsimd.alloc_register("stage_rv")
        for k in range(NSLOT):
            span = LHAT[k] * 128
            wk = self.internal[f"walk_k{k}"]
            wv = self.internal[f"walk_v{k}"]

            def stage_k(e, k=k, span=span, wk=wk):
                e.reg_load(r_sc, self.meta_sb[0:1, 8 + k:9 + k])
                return e.dma_start(
                    out=bass.AP(wk, 0, [[128 * span, 8], [span, 128], [1, span]]),
                    in_=bass.AP(kd, r_sc, [[128 * KCOLS, 8], [KCOLS, 128], [1, span]]))
            tks = kb.emit("sc", stage_k, waits=[zone_tok, last_kv_store],
                          inc="dSC", inc_n=16)

            def stage_v(e, k=k, span=span, wv=wv):
                e.reg_load(r_ge, self.meta_sb[0:1, 12 + k:13 + k])
                return e.dma_start(out=bass.AP(wv, 0, [[1, span * V65]]),
                                   in_=bass.AP(vd, r_ge, [[1, span * V65]]))
            tvs = kb.emit("ge", stage_v, waits=[zone_tok, last_kv_store],
                          inc="dGE", inc_n=16)
            stage_tok[k] = (tks, tvs)
        gidx = 0
        for k in range(NSLOT):
            span = LHAT[k] * 128
            wk = self.internal[f"walk_k{k}"]
            wv = self.internal[f"walk_v{k}"]
            for hp in range(8):
                ngrp = GB[k]
                pv_last = None
                psa_half_free = {0: psa_free, 1: psa_free}
                for g in range(ngrp):
                    bk = gidx % 3
                    bp = gidx % 2
                    koff = hp * 128 * span + span - 256 * (g + 1)
                    tk = kb.dma("sy", kbuf[bk][:, :],
                                bass.AP(wk, koff, [[span, 128], [1, 256]]),
                                waits=[kfree[bk], stage_tok[k][0]])
                    voff = (span - 256 * (g + 1)) * V65 + hp * 130
                    tv = kb.dma("ge", vbuf[bk][:, :],
                                bass.AP(wv, voff, [[V65, 128], [128 * V65, 2], [1, 130]]),
                                waits=[vfree[bk], stage_tok[k][1]])
                    smt_h = {}
                    for tt in range(2):
                        for hh in range(2):
                            col = (2 * tt + hh) * 512
                            lhsT = kbuf[bk][hh * 64:hh * 64 + 64,
                                            tt * 128:tt * 128 + 128]
                            rhs = self.q_own[hh * 64:hh * 64 + 64,
                                             hp * TOK + k * 512:hp * TOK + k * 512 + 512]
                            w = []
                            if tt == 0 and hh == 0:
                                w = [tk, psa_half_free[0]]
                            elif tt == 1 and hh == 0:
                                w = [psa_half_free[1]]
                            smt = self.mm(self.psa[:, col:col + 512], lhsT, rhs,
                                          True, True, w, inc=(hh == 1))
                            if hh == 1:
                                smt_h[tt] = smt
                    kfree[bk] = smt_h[1]
                    te_h = {}
                    tmn_h = {}
                    for hf in range(2):
                        te_h[hf] = kb.op("sc", lambda e, bp=bp, hf=hf: e.activation(
                            pbuf[bp][:, hf * 1024:hf * 1024 + 1024],
                            self.psa[:, hf * 1024:hf * 1024 + 1024], AF.Exp,
                            bias=L0_BIAS, scale=QSCALE),
                            waits=[smt_h[hf], pfree[bp]])
                        psa_half_free[hf] = te_h[hf]
                        if g < 2:
                            msk = self.mask0 if g == 0 else self.mask1
                            tmn_h[hf] = kb.op("ve", lambda e, bp=bp, hf=hf, msk=msk:
                                              e.tensor_tensor(
                                pbuf[bp][:, hf * 1024:hf * 1024 + 1024],
                                pbuf[bp][:, hf * 1024:hf * 1024 + 1024],
                                msk[:, hf * 1024:hf * 1024 + 1024], ALU.min),
                                waits=[te_h[hf]])
                        else:
                            tmn_h[hf] = kb.op("ve", lambda e, bp=bp, hf=hf:
                                              e.tensor_scalar(
                                pbuf[bp][:, hf * 1024:hf * 1024 + 1024],
                                pbuf[bp][:, hf * 1024:hf * 1024 + 1024],
                                1.0, None, ALU.min),
                                waits=[te_h[hf]])
                    pvt = None
                    for tt in range(2):
                        for hh in range(2):
                            col = (2 * tt + hh) * 512
                            lhsT = vbuf[bk][:, tt * 130 + hh * 65:tt * 130 + hh * 65 + 65]
                            rhs = pbuf[bp][:, col:col + 512]
                            st_f = (g == 0 and tt == 0)
                            sp_f = (g == ngrp - 1 and tt == 1)
                            w = [tmn_h[tt], tv]
                            if st_f:
                                w = w + o_guard[hh]
                            pvt = self.mm(self.pb[hh][0:65, :], lhsT, rhs,
                                          st_f, sp_f, w, inc=(tt == 1 and hh == 1))
                    vfree[bk] = pvt
                    pfree[bp] = pvt
                    pv_last = pvt
                    gidx += 1
                for hh in range(2):
                    rr = self.srow["rr"][0:1, :]
                    trr = kb.op("ve", lambda e, hh=hh: e.reciprocal(
                        rr, self.pb[hh][64:65, 0:512]),
                        waits=[pv_last] + kb.wg("srow_rr"))
                    trp = self.mm(self.pb[2][0:64, :], self.ones1[0:1, 0:64], rr,
                                  True, True, waits=[trr] + kb.wg("pb2"), inc=True)
                    kb.rd("srow_rr", trp)
                    tco = kb.op("ve", lambda e, hh=hh: e.tensor_copy(
                        osb[:, :], self.pb[hh][0:64, 0:512]), waits=[pv_last])
                    tat = kb.op("ve", lambda e, hh=hh, hp=hp, k=k: e.tensor_tensor(
                        self.attnout[hh * 64:hh * 64 + 64,
                                     hp * TOK + k * 512:hp * TOK + k * 512 + 512],
                        osb[:, :], self.pb[2][0:64, :], ALU.mult),
                        waits=[tco, trp])
                    kb.rd("pb2", tat)
                    o_guard[hh] = [tco]
                psa_free = psa_half_free[1]
                if os.environ.get("KPAR2", "0") == "1":
                    kb.barrier()
        attnout_tok = kb.op("ve", lambda e: e.nop())

        kb.end_scratch()
        kb.begin_scratch()
        wo_sb = kb.sb("wo_sb", [128, 8 * DM], BF16, scratch=True)
        bo_f = kb.sb("bo_f", [128, 4096], F32, scratch=True)
        xt_f = kb.sb("xt_f", [128, 4096], F32, scratch=True)
        sq_c = kb.sb("sq_c", [128, 4096], F32, scratch=True)
        tmp_c = kb.sb("tmp_c", [128, 512], F32, scratch=True)
        wo = self.inputs["woutT"]
        wo_tok = None
        for dc in range(8):
            wo_tok = kb.dma("ge", wo_sb[:, dc * DM:(dc + 1) * DM],
                            bass.AP(wo, dc * 128 * DM, [[DM, 128], [1, DM]]),
                            waits=[attnout_tok])
        for c in range(8):
            kb.rd(("xt", c), attnout_tok)
            kb.rd(("bo", c), attnout_tok)
        for i in range(8):
            kb.rd(("sq", i), attnout_tok)
        self._proj_ln_resid(self.attnout, attnout_tok, TOK, wo_sb, wo_tok, DM, 8,
                            bo_f, xt_f, sq_c, tmp_c, self.xin,
                            self.xdst_of["l0attn"], None, ev)
        kb.end_scratch()
        kb.end_scratch()

    def _proj_ln_resid(self, pin, pin_tok, pin_stride, w_sb, w_tok, w_stride, n_k,
                       bo_f, xt_f, sq_t, tmp_f, xsrc, xdst, scale, ev):
        kb = self.kb
        for ti in range(4):
            xls = []
            for c in range(8):
                xls.append(kb.dma("sy", xt_f[:, c * 512:(c + 1) * 512],
                                  bass.AP(xsrc, c * 128 * TOK + ti * 512,
                                          [[TOK, 128], [1, 512]]),
                                  waits=kb.wg(("xt", c))))
            bo_toks = []
            for ec in range(8):
                bank = ec % 4
                ps = self.psa[:, bank * 512:bank * 512 + 512]
                mt = None
                for dc in range(n_k):
                    lhsT = w_sb[:, dc * w_stride + ec * 128:dc * w_stride + ec * 128 + 128]
                    rhs = pin[:, dc * pin_stride + ti * 512:dc * pin_stride + ti * 512 + 512]
                    w = ([w_tok, pin_tok, ev[bank]] if dc == 0 else [])
                    mt = self.mm(ps, lhsT, rhs, dc == 0, dc == n_k - 1, w,
                                 inc=(dc == n_k - 1))
                if scale is None:
                    cp = kb.op("ve", lambda e, ps=ps, ec=ec: e.tensor_copy(
                        bo_f[:, ec * 512:(ec + 1) * 512], ps),
                        waits=[mt] + kb.wg(("bo", ec)))
                else:
                    cp = kb.op("ve", lambda e, ps=ps, ec=ec: e.tensor_scalar(
                        bo_f[:, ec * 512:(ec + 1) * 512], ps, scale, None, ALU.mult),
                        waits=[mt] + kb.wg(("bo", ec)))
                ev[bank] = cp
                bo_toks.append(cp)
            tiles = [(bo_f[:, c * 512:(c + 1) * 512], bo_toks[c]) for c in range(8)]
            m, r, tmr, _ = self.ln_stats(tiles, sq_t)
            A, B, tA, tB = self.bcast_pair(m, r, tmr)
            lnt = self.ln_apply(tiles, A, B, tA, tB,
                                lambda i: bo_f[:, i * 512:(i + 1) * 512], tmp_f)
            for c in range(8):
                kb.rd(("bo", c), lnt[c])
                t3 = kb.op("ve", lambda e, c=c: e.tensor_tensor(
                    xt_f[:, c * 512:(c + 1) * 512], xt_f[:, c * 512:(c + 1) * 512],
                    bo_f[:, c * 512:(c + 1) * 512], ALU.add),
                    waits=[lnt[c], xls[c]])
                kb.rd(("bo", c), t3)
                st = kb.dma("sc", bass.AP(xdst, c * 128 * TOK + ti * 512,
                                          [[TOK, 128], [1, 512]]),
                            xt_f[:, c * 512:(c + 1) * 512], waits=[t3])
                kb.rd(("xt", c), st)

    def emit_ffn(self, tag, xsrc, xdst):
        kb = self.kb
        kb.begin_scratch()
        wgT, wuT, wdT = (self.inputs[f"{tag}{s}T"] for s in ("g", "u", "d"))
        wg_sb = [kb.sb(f"ffn{tag}_wg{i}", [128, 1024], BF16, scratch=True) for i in range(2)]
        wu_sb = [kb.sb(f"ffn{tag}_wu{i}", [128, 1024], BF16, scratch=True) for i in range(2)]
        wd_sb = [kb.sb(f"ffn{tag}_wd{i}", [128, 3072], BF16, scratch=True) for i in range(2)]
        xln = kb.sb(f"ffn{tag}_xln", [128, 8 * 1024], BF16, scratch=True)
        h_sb = kb.sb(f"ffn{tag}_h", [128, 24 * 1024], BF16, scratch=True)
        g_sb = kb.sb(f"ffn{tag}_g", [128, 1024], BF16, scratch=True)
        u_sb = kb.sb(f"ffn{tag}_u", [128, 1024], BF16, scratch=True)
        sq_t = kb.sb(f"ffn{tag}_sq", [128, 4096], F32, scratch=True)
        tmp_f = kb.sb(f"ffn{tag}_tmp", [128, 512], F32, scratch=True)
        bo_f = kb.sb(f"ffn{tag}_bo", [128, 4096], F32, scratch=True)
        xt_f = kb.sb(f"ffn{tag}_xt", [128, 4096], F32, scratch=True)

        wfree = {"g": [None, None], "u": [None, None], "d": [None, None]}
        ev = {i: None for i in range(4)}
        for sup in range(2):
            xln_toks = []
            for sub in range(2):
                ti = sup * 2 + sub
                xls = []
                for c in range(8):
                    xls.append(kb.dma("sy", xt_f[:, c * 512:(c + 1) * 512],
                                      bass.AP(xsrc, c * 128 * TOK + ti * 512,
                                              [[TOK, 128], [1, 512]]),
                                      waits=kb.wg(("xt", c))))
                tiles = [(xt_f[:, c * 512:(c + 1) * 512], xls[c]) for c in range(8)]
                m, r, tmr, _ = self.ln_stats(tiles, sq_t)
                A, B, tA, tB = self.bcast_pair(m, r, tmr)
                toks = self.ln_apply(tiles, A, B, tA, tB,
                                     lambda i, sub=sub: xln[:, i * 1024 + sub * 512:
                                                           i * 1024 + sub * 512 + 512],
                                     tmp_f)
                for c in range(8):
                    kb.rd(("xt", c), toks[c])
                xln_toks.append(toks[-1])
            xln_tok = xln_toks[-1]

            h_tok = None
            for hc in range(24):
                ib = hc % 2
                tg = kb.dma("ge", wg_sb[ib][:, :],
                            bass.AP(wgT, hc * 128, [[HID, 128], [HID * 128, 8], [1, 128]]),
                            waits=[wfree["g"][ib]])
                tu = kb.dma("ge", wu_sb[ib][:, :],
                            bass.AP(wuT, hc * 128, [[HID, 128], [HID * 128, 8], [1, 128]]),
                            waits=[wfree["u"][ib]])
                gmt = umt = None
                for half in range(2):
                    for dc in range(8):
                        lhsT = wg_sb[ib][:, dc * 128:(dc + 1) * 128]
                        rhs = xln[:, dc * 1024 + half * 512:dc * 1024 + half * 512 + 512]
                        w = ([tg, xln_tok] + kb.wg("psa01")) if (dc == 0 and half == 0) else []
                        gmt = self.mm(self.psa[:, half * 512:half * 512 + 512],
                                      lhsT, rhs, dc == 0, dc == 7, w,
                                      inc=(dc == 7 and half == 1))
                for half in range(2):
                    for dc in range(8):
                        lhsT = wu_sb[ib][:, dc * 128:(dc + 1) * 128]
                        rhs = xln[:, dc * 1024 + half * 512:dc * 1024 + half * 512 + 512]
                        w = ([tu] + kb.wg("psa23")) if (dc == 0 and half == 0) else []
                        umt = self.mm(self.psa[:, 1024 + half * 512:1536 + half * 512],
                                      lhsT, rhs, dc == 0, dc == 7, w,
                                      inc=(dc == 7 and half == 1))
                wfree["g"][ib] = gmt
                wfree["u"][ib] = umt
                tsl = kb.op("sc", lambda e: e.activation(
                    g_sb[:, :], self.psa[:, 0:1024], AF.Silu, scale=FS),
                    waits=[gmt] + kb.wg("g_sb"))
                kb.rd("psa01", tsl)
                tuc = kb.op("ve", lambda e: e.tensor_copy(u_sb[:, :], self.psa[:, 1024:2048]),
                            waits=[umt] + kb.wg("u_sb"))
                kb.rd("psa23", tuc)
                th = kb.op("ve", lambda e, hc=hc: e.tensor_tensor(
                    h_sb[:, hc * 1024:(hc + 1) * 1024], g_sb[:, :], u_sb[:, :], ALU.mult),
                    waits=[tsl, tuc] + kb.wg(("h", hc)))
                kb.rd("g_sb", th)
                kb.rd("u_sb", th)
                h_tok = th
                if os.environ.get("KPAR2", "0") == "1" and hc % 6 == 5:
                    kb.barrier()

            for sub in range(2):
                ti = sup * 2 + sub
                bo_toks = []
                for ec in range(8):
                    ib = ec % 2
                    td = kb.dma("ge", wd_sb[ib][:, :],
                                bass.AP(wdT, ec * 128, [[DM, 128], [DM * 128, 24], [1, 128]]),
                                waits=[wfree["d"][ib]])
                    bank = ec % 4
                    ps = self.psa[:, bank * 512:bank * 512 + 512]
                    mt = None
                    for hc in range(24):
                        lhsT = wd_sb[ib][:, hc * 128:(hc + 1) * 128]
                        rhs = h_sb[:, hc * 1024 + sub * 512:hc * 1024 + sub * 512 + 512]
                        w = ([td, h_tok, ev[bank]] if hc == 0 else [])
                        mt = self.mm(ps, lhsT, rhs, hc == 0, hc == 23, w, inc=(hc == 23))
                    wfree["d"][ib] = mt
                    for hc in range(24):
                        kb.rd(("h", hc), mt)
                    cp = kb.op("ve", lambda e, ps=ps, ec=ec: e.tensor_scalar(
                        bo_f[:, ec * 512:(ec + 1) * 512], ps, FS2, None, ALU.mult),
                        waits=[mt] + kb.wg(("bo", ec)))
                    ev[bank] = cp
                    bo_toks.append(cp)
                xls = []
                for c in range(8):
                    xls.append(kb.dma("sy", xt_f[:, c * 512:(c + 1) * 512],
                                      bass.AP(xsrc, c * 128 * TOK + ti * 512,
                                              [[TOK, 128], [1, 512]]),
                                      waits=kb.wg(("xt", c))))
                tiles = [(bo_f[:, c * 512:(c + 1) * 512], bo_toks[c]) for c in range(8)]
                m, r, tmr, _ = self.ln_stats(tiles, sq_t)
                A, B, tA, tB = self.bcast_pair(m, r, tmr)
                lnt = self.ln_apply(tiles, A, B, tA, tB,
                                    lambda i: bo_f[:, i * 512:(i + 1) * 512], tmp_f)
                for c in range(8):
                    kb.rd(("bo", c), lnt[c])
                    t3 = kb.op("ve", lambda e, c=c: e.tensor_tensor(
                        xt_f[:, c * 512:(c + 1) * 512], xt_f[:, c * 512:(c + 1) * 512],
                        bo_f[:, c * 512:(c + 1) * 512], ALU.add),
                        waits=[lnt[c], xls[c]])
                    kb.rd(("bo", c), t3)
                    st = kb.dma("sc", bass.AP(xdst, c * 128 * TOK + ti * 512,
                                              [[TOK, 128], [1, 512]]),
                                xt_f[:, c * 512:(c + 1) * 512], waits=[t3])
                    kb.rd(("xt", c), st)
                if os.environ.get("KPAR2", "0") == "1":
                    kb.barrier()
        kb.end_scratch()

    def emit_l1attn(self, xsrc, xdst):
        kb = self.kb
        kb.begin_scratch()
        w_sb = kb.sb("l1w_sb", [128, 8 * 3 * DM], BF16, scratch=True)
        xt_f = kb.sb("l1_xt", [128, 4096], F32, scratch=True)
        sq_t = kb.sb("l1_sq", [128, 4096], F32, scratch=True)
        tmp_f = kb.sb("l1_tmp", [128, 512], F32, scratch=True)
        xlf = kb.sb("l1_xlf", [128, 4096], F32, scratch=True)
        xlb = kb.sb("l1_xlb", [128, 4096], BF16, scratch=True)
        xq = kb.sb("l1_xq", [128, 4096], BF16, scratch=True)
        xqs = kb.sb("l1_xqs", [128, 4096], BF16, scratch=True)
        xk = kb.sb("l1_xk", [128, 4096], BF16, scratch=True)
        xv = kb.sb("l1_xv", [128, 4 * 1040], BF16, scratch=True)
        pbuf = [kb.sb(f"l1_p{i}", [128, 2048], BF16, scratch=True) for i in range(2)]
        osb = kb.sb("l1_osb", [128, 512], F32, scratch=True)
        ysb = kb.sb("l1_y", [128, 512], F32, scratch=True)
        ysq = kb.sb("l1_ysq", [128, 512], F32, scratch=True)
        yd = self.internal["y_dram"]

        wl = self.inputs["l1wT"]
        w_tok = None
        for dc in range(8):
            w_tok = kb.dma("ge", w_sb[:, dc * 3 * DM:(dc + 1) * 3 * DM],
                           bass.AP(wl, dc * 128 * 3 * DM, [[3 * DM, 128], [1, 3 * DM]]))
        xv_ones = kb.op("ve", lambda e: e.memset(xv[:, :], 1.0))

        prev_ev = {i: None for i in range(4)}
        psa_free = None
        pfree = [None, None]
        y_store_last = None
        gidx = 0
        for ck in range(4):
            xls = []
            for c in range(8):
                xls.append(kb.dma("sy", xt_f[:, c * 512:(c + 1) * 512],
                                  bass.AP(xsrc, c * 128 * TOK + ck * 512,
                                          [[TOK, 128], [1, 512]]),
                                  waits=kb.wg(("xt", c))))
            tiles = [(xt_f[:, c * 512:(c + 1) * 512], xls[c]) for c in range(8)]
            m, r, tmr, _ = self.ln_stats(tiles, sq_t)
            A, B, tA, tB = self.bcast_pair(m, r, tmr)
            lnt = self.ln_apply(tiles, A, B, tA, tB,
                                lambda i: xlf[:, i * 512:(i + 1) * 512], tmp_f)
            xl_toks = []
            for c in range(8):
                kb.rd(("xt", c), lnt[c])
                t3 = kb.op("ve", lambda e, c=c: e.tensor_copy(
                    xlb[:, c * 512:(c + 1) * 512], xlf[:, c * 512:(c + 1) * 512]),
                    waits=[lnt[c]] + kb.wg(("xlb", c)))
                xl_toks.append(t3)
            xln_tok = xl_toks[-1]

            def proj(sec, out_sb, outkey):
                toks = []
                for ec in range(8):
                    bank = ec % 4
                    ps = self.psa[:, bank * 512:bank * 512 + 512]
                    mt = None
                    for dc in range(8):
                        lhsT = w_sb[:, dc * 3 * DM + sec * DM + ec * 128:
                                    dc * 3 * DM + sec * DM + ec * 128 + 128]
                        rhs = xlb[:, dc * 512:(dc + 1) * 512]
                        w = ([w_tok, xln_tok, prev_ev[bank], psa_free]
                             if dc == 0 else [])
                        mt = self.mm(ps, lhsT, rhs, dc == 0, dc == 7, w, inc=(dc == 7))
                    cp = kb.op("ve", lambda e, ps=ps, ec=ec, o=out_sb: e.tensor_copy(
                        o[:, ec * 512:(ec + 1) * 512], ps),
                        waits=[mt] + kb.wg((outkey, ec)))
                    prev_ev[bank] = cp
                    toks.append(cp)
                return toks

            q_toks = proj(0, xq, "xq")
            k_toks = proj(1, xk, "xk")
            ts1 = kb.dma("ge", xqs[0:64, :], xq[64:128, :],
                         waits=[q_toks[-1]] + kb.wg("xqs"))
            ts2 = kb.dma("ge", xqs[64:128, :], xq[0:64, :], waits=[q_toks[-1]])
            xqs_tok = ts2
            v_tok = None
            for ts_ in range(4):
                for half in range(2):
                    bank = (ts_ * 2 + half) % 4
                    ps = self.psa[:, bank * 512:bank * 512 + 512]
                    mt = None
                    for dc in range(8):
                        lhsT = xlb[:, dc * 512 + ts_ * 128:dc * 512 + ts_ * 128 + 128]
                        rhs = w_sb[:, dc * 3 * DM + 2 * DM + half * 512:
                                   dc * 3 * DM + 2 * DM + half * 512 + 512]
                        w = ([xln_tok, prev_ev[bank]] if dc == 0 else [])
                        mt = self.mm(ps, lhsT, rhs, dc == 0, dc == 7, w, inc=(dc == 7))
                    out_ap = bass.AP(xv, ts_ * 1040 + half * 520,
                                     [[4160, 128], [65, 8], [1, 64]])
                    in_ap = bass.AP(ps.tensor, ps.offset, [[2048, 128], [64, 8], [1, 64]])
                    cp = kb.op("ve", lambda e, o=out_ap, i_=in_ap: e.tensor_copy(o, i_),
                               waits=[mt, xv_ones] + kb.wg(("xv", ts_ * 2 + half)))
                    prev_ev[bank] = cp
                    v_tok = cp

            for g in range(16):
                pv_last = None
                og = kb.wg("pb0")
                psa_half_free = {0: psa_free, 1: psa_free}
                for u in range(16):
                    bp = gidx % 2
                    smt_h = {}
                    for tt in range(4):
                        lhsT = xk[(u % 2) * 64:(u % 2) * 64 + 64,
                                  (u // 2) * 512 + tt * 128:(u // 2) * 512 + tt * 128 + 128]
                        src = xq if (u % 2) == (g % 2) else xqs
                        rhs = src[(u % 2) * 64:(u % 2) * 64 + 64,
                                  (g // 2) * 512:(g // 2) * 512 + 512]
                        w = []
                        if tt == 0:
                            w = [k_toks[-1], v_tok, xqs_tok, psa_half_free[0]]
                        elif tt == 2:
                            w = [psa_half_free[1]]
                        smt = self.mm(self.psa[:, tt * 512:tt * 512 + 512],
                                      lhsT, rhs, True, True, w,
                                      inc=(tt == 1 or tt == 3))
                        if tt == 1 or tt == 3:
                            smt_h[tt // 2] = smt
                    te_h = {}
                    for hf in range(2):
                        te_h[hf] = kb.op("sc", lambda e, bp=bp, hf=hf: e.activation(
                            pbuf[bp][:, hf * 1024:hf * 1024 + 1024],
                            self.psa[:, hf * 1024:hf * 1024 + 1024],
                            AF.Exp, bias=L1_BIAS),
                            waits=[smt_h[hf], pfree[bp]])
                        psa_half_free[hf] = te_h[hf]
                    pvt = None
                    for tt in range(4):
                        lhsT = xv[:, tt * 1040 + u * 65:tt * 1040 + u * 65 + 65]
                        rhs = pbuf[bp][:, tt * 512:tt * 512 + 512]
                        st_f = (u == 0 and tt == 0)
                        sp_f = (u == 15 and tt == 3)
                        w = [te_h[tt // 2]] + (og if st_f else [])
                        pvt = self.mm(self.pb[0][0:65, :], lhsT, rhs, st_f, sp_f,
                                      w, inc=(tt == 3))
                    pfree[bp] = pvt
                    pv_last = pvt
                    gidx += 1
                par = (g % 2) * 64
                osl = osb[par:par + 64, :]
                ysl = ysb[par:par + 64, :]
                yql = ysq[par:par + 64, :]
                rr = self.srow["rr"][0:1, :]
                trr = kb.op("ve", lambda e: e.reciprocal(rr, self.pb[0][64:65, 0:512]),
                            waits=[pv_last] + kb.wg("srow_rr"))
                trp = self.mm(self.pb[2][0:64, :], self.ones1[0:1, 0:64], rr,
                              True, True, waits=[trr] + kb.wg("pb2"), inc=True)
                kb.rd("srow_rr", trp)
                tco = kb.op("ve", lambda e, osl=osl: e.tensor_copy(
                    osl, self.pb[0][0:64, 0:512]),
                    waits=[pv_last] + kb.wg("osb"))
                kb.rd("pb0", tco)
                kb.rd("pb0", trr)
                tat = kb.op("ve", lambda e, osl=osl: e.tensor_tensor(
                    osl, osl, self.pb[2][0:64, :], ALU.mult),
                    waits=[tco, trp])
                kb.rd("pb2", tat)
                xlsl = xlf[par:par + 64, (g // 2) * 512:(g // 2) * 512 + 512]
                ty = kb.op("ve", lambda e, xlsl=xlsl, ysl=ysl, osl=osl: e.tensor_tensor(
                    ysl, osl, xlsl, ALU.add),
                    waits=[tat] + kb.wg("ysb"))
                kb.rd("osb", ty)
                tq2 = kb.op("ge", lambda e, ysl=ysl, yql=yql: e.tensor_tensor(
                    yql, ysl, ysl, ALU.mult),
                    waits=[ty] + kb.wg("ysq"))
                kb.rd("ysb", tq2)
                t_s1 = self.mm(self.pb[1][0:1, 0:512], self.ones128[par:par + 64, 0:1],
                               ysl, True, True,
                               waits=[ty] + kb.wg("pb1"), inc=True)
                kb.rd("ysb", t_s1)
                t_s2 = self.mm(self.pb[3][0:1, 0:512], self.ones128[par:par + 64, 0:1],
                               yql, True, True,
                               waits=[tq2] + kb.wg("pb3"), inc=True)
                kb.rd("ysq", t_s2)
                tr1 = kb.op("ve", lambda e: e.tensor_reduce(
                    self.gstat[0:1, 8:9], self.pb[1][0:1, 0:512], AX.X, ALU.add),
                    waits=[t_s1])
                kb.rd("pb1", tr1)
                tr2 = kb.op("ve", lambda e: e.tensor_reduce(
                    self.gstat[0:1, 9:10], self.pb[3][0:1, 0:512], AX.X, ALU.add),
                    waits=[t_s2])
                kb.rd("pb3", tr2)
                ta1 = kb.op("ve", lambda e: e.tensor_tensor(
                    self.gstat[0:1, 0:1], self.gstat[0:1, 0:1], self.gstat[0:1, 8:9],
                    ALU.add), waits=[tr1])
                ta2 = kb.op("ve", lambda e: e.tensor_tensor(
                    self.gstat[0:1, 1:2], self.gstat[0:1, 1:2], self.gstat[0:1, 9:10],
                    ALU.add), waits=[tr2])
                self.g_acc_tok = ta2
                yst = kb.dma("sc", bass.AP(yd, (g * 64) * TOK + ck * 512,
                                           [[TOK, 64], [1, 512]]),
                             ysl, waits=[ty])
                kb.rd("ysb", yst)
                y_store_last = yst
                if os.environ.get("KPAR2", "0") == "1":
                    kb.barrier()

        if os.environ.get("KDBG_YTAP"):
            dbgy = self.nc.dram_tensor("dbg_y", [DM, TOK], F32, kind="ExternalOutput")
            for c8 in range(8):
                kb.dma("sc", bass.AP(dbgy, c8 * 128 * TOK, [[TOK, 128], [1, TOK]]),
                       bass.AP(yd, c8 * 128 * TOK, [[TOK, 128], [1, TOK]]),
                       waits=[y_store_last])
        gs = self.gstat
        cin = self.internal["cc_in"]
        cout = self.internal["cc_out"]
        tci = kb.dma("ge", cin.ap(), gs[0:1, 0:2], waits=[self.g_acc_tok])
        tcc = kb.emit("ge", lambda e: e.collective_compute(
            "AllReduce", ALU.add, replica_groups=[[0, 1, 2, 3], [4, 5, 6, 7]],
            ins=[cin.ap()], outs=[cout.ap()]),
            waits=[tci], inc="CC", inc_n=1)
        tco2 = kb.emit("ge", lambda e: e.dma_start(out=gs[0:1, 2:4], in_=cout.ap()),
                       waits=[tcc], inc="dGE", inc_n=16)
        t1 = kb.op("ve", lambda e: e.tensor_scalar(gs[0:1, 4:5], gs[0:1, 2:3],
                                                   INV_NG, None, ALU.mult), waits=[tco2])
        t2 = kb.op("ve", lambda e: e.tensor_scalar(gs[0:1, 5:6], gs[0:1, 3:4],
                                                   INV_NG, None, ALU.mult), waits=[tco2])
        t3 = kb.op("ve", lambda e: e.tensor_tensor(gs[0:1, 4:5], gs[0:1, 4:5],
                                                   gs[0:1, 4:5], ALU.mult), waits=[t1])
        t4 = kb.op("ve", lambda e: e.tensor_tensor(gs[0:1, 5:6], gs[0:1, 5:6],
                                                   gs[0:1, 4:5], ALU.subtract),
                   waits=[t2, t3])
        t5 = kb.op("sc", lambda e: e.activation(gs[0:1, 5:6], gs[0:1, 5:6],
                                                AF.Sqrt, bias=EPS), waits=[t4])
        t6 = kb.op("ve", lambda e: e.reciprocal(gs[0:1, 6:7], gs[0:1, 5:6]), waits=[t5])
        rg_tok = kb.op("ve", lambda e: e.tensor_tensor(gs[0:1, 7:8], gs[0:1, 6:7],
                                                       gs[0:1, 6:7], ALU.mult),
                       waits=[t6])

        yt_f = xlf
        for ti in range(4):
            yls = []
            for c in range(8):
                yls.append(kb.dma("sy", yt_f[:, c * 512:(c + 1) * 512],
                                  bass.AP(yd, c * 128 * TOK + ti * 512,
                                          [[TOK, 128], [1, 512]]),
                                  waits=[y_store_last] + kb.wg(("yt", c))))
            xls = []
            for c in range(8):
                xls.append(kb.dma("sy", xt_f[:, c * 512:(c + 1) * 512],
                                  bass.AP(xsrc, c * 128 * TOK + ti * 512,
                                          [[TOK, 128], [1, 512]]),
                                  waits=kb.wg(("xt", c))))
            tiles = [(yt_f[:, c * 512:(c + 1) * 512], yls[c]) for c in range(8)]
            sq_toks = []
            for i, (ya, yt) in enumerate(tiles):
                sqa = sq_t[:, i * 512:(i + 1) * 512]
                sq_toks.append(kb.op("ge", lambda e, ya=ya, sqa=sqa:
                                     e.tensor_tensor(sqa, ya, ya, ALU.mult),
                                     waits=[yt] + kb.wg(("sq", i))))
            s_tok = q_tok = None
            for i, (ya, yt) in enumerate(tiles):
                s_tok = self.mm(self.pb[3][0:1, 0:512], self.ones128[:, :], ya,
                                i == 0, i == 7,
                                waits=[yt] + (kb.wg("pb3") if i == 0 else []),
                                inc=(i == 7))
            for i in range(8):
                sqa = sq_t[:, i * 512:(i + 1) * 512]
                q_tok = self.mm(self.pb[1][0:1, 0:512], self.ones128[:, :], sqa,
                                i == 0, i == 7,
                                waits=[sq_toks[i]] + (kb.wg("pb1") if i == 0 else []),
                                inc=(i == 7))
            for i in range(8):
                kb.rd(("sq", i), q_tok)
            mrow, qrow = self.srow["m"][0:1, :], self.srow["q"][0:1, :]
            urow, trow, crow, mcrow = (self.srow[k][0:1, :]
                                       for k in ("u", "t", "c", "mc"))
            tm = kb.op("ve", lambda e: e.tensor_scalar(mrow, self.pb[3][0:1, 0:512],
                                                       INV_D, None, ALU.mult),
                       waits=[s_tok] + kb.wg("srow_m"))
            tq = kb.op("ve", lambda e: e.tensor_scalar(qrow, self.pb[1][0:1, 0:512],
                                                       INV_D, None, ALU.mult),
                       waits=[q_tok] + kb.wg("srow_q"))
            kb.rd("pb3", tm)
            kb.rd("pb1", tq)
            tu = kb.op("ve", lambda e: e.tensor_tensor(urow, mrow, mrow, ALU.mult),
                       waits=[tm] + kb.wg("srow_u"))
            tu = kb.op("ve", lambda e: e.tensor_tensor(urow, qrow, urow, ALU.subtract),
                       waits=[tu, tq])
            tt_ = kb.op("ve", lambda e: e.tensor_scalar(trow, urow, gs[0:1, 7:8],
                                                        EPS, ALU.mult, ALU.add),
                        waits=[tu, rg_tok] + kb.wg("srow_t"))
            ts_ = kb.op("sc", lambda e: e.activation(trow, trow, AF.Sqrt), waits=[tt_])
            tc = kb.op("ve", lambda e: e.reciprocal(crow, trow), waits=[ts_] + kb.wg("srow_c"))
            kb.rd("srow_t", tc)
            tc = kb.op("ve", lambda e: e.tensor_scalar(crow, crow, gs[0:1, 6:7],
                                                       None, ALU.mult), waits=[tc])
            tmc = kb.op("ve", lambda e: e.tensor_tensor(mcrow, mrow, crow, ALU.mult),
                        waits=[tc] + kb.wg("srow_mc"))
            A, B, tA, tB = self.bcast_pair(crow, mcrow, tmc)
            for c in range(8):
                yc = yt_f[:, c * 512:(c + 1) * 512]
                t1_ = kb.op("ve", lambda e, yc=yc: e.tensor_tensor(
                    tmp_f[:, :], yc, A, ALU.mult), waits=[tA, yls[c]])
                t2_ = kb.op("ve", lambda e, yc=yc: e.tensor_tensor(
                    yc, tmp_f[:, :], B, ALU.subtract), waits=[t1_, tB])
                kb.rd(("yt", c), t2_)
                kb.rd("pb2", t1_)
                kb.rd("pb1", t2_)
                t3_ = kb.op("ve", lambda e, c=c, yc=yc: e.tensor_tensor(
                    xt_f[:, c * 512:(c + 1) * 512], xt_f[:, c * 512:(c + 1) * 512],
                    yc, ALU.add), waits=[t2_, xls[c]])
                kb.rd(("yt", c), t3_)
                st = kb.dma("sc", bass.AP(xdst, c * 128 * TOK + ti * 512,
                                          [[TOK, 128], [1, 512]]),
                            xt_f[:, c * 512:(c + 1) * 512], waits=[t3_])
                kb.rd(("xt", c), st)
            if os.environ.get("KPAR2", "0") == "1":
                kb.barrier()
        kb.end_scratch()

    def build(self):
        ph = self.phases
        prev = None
        for p in ph:
            src = self.xin if prev is None else self.xdst_of[prev]
            if prev is not None:
                self.kb.barrier()
            if p == "l0attn":
                assert src is self.xin
                self.emit_l0attn()
            elif p == "l0ffn":
                self.emit_ffn("l0", src, self.xdst_of[p])
            elif p == "l1attn":
                self.emit_l1attn(src, self.xdst_of[p])
            elif p == "l1ffn":
                self.emit_ffn("l1", src, self.xdst_of[p])
            prev = p
        self.kb._dbg_internal = self.internal
        self.kb.finalize()
        return self.nc


def _prep(inputs):
    x = np.ascontiguousarray(inputs["L0_attn_x"]).astype(np.float32, copy=False)
    shared = {
        "consts": np.tile(np.array([CONST_VALS], dtype=np.float32), (128, 1)),
        "wqkvT": np.ascontiguousarray(np.asarray(inputs["L0_attn_Wqkv"]).T),
        "woutT": np.ascontiguousarray(np.asarray(inputs["L0_attn_Wout"]).T),
        "l1wT": np.ascontiguousarray(
            np.concatenate([np.asarray(inputs["L1_attn_Wq"]),
                            np.asarray(inputs["L1_attn_Wk"]),
                            np.asarray(inputs["L1_attn_Wv"])], axis=0).T),
        "l0gT": np.ascontiguousarray(np.asarray(inputs["L0_ffn_w_gate"]).T),
        "l0uT": np.ascontiguousarray(np.asarray(inputs["L0_ffn_w_up"]).T),
        "l0dT": np.ascontiguousarray(np.asarray(inputs["L0_ffn_w_down"]).T),
        "l1gT": np.ascontiguousarray(np.asarray(inputs["L1_ffn_w_gate"]).T),
        "l1uT": np.ascontiguousarray(np.asarray(inputs["L1_ffn_w_up"]).T),
        "l1dT": np.ascontiguousarray(np.asarray(inputs["L1_ffn_w_down"]).T),
        "mask0": np.ascontiguousarray(mask_tile_np(0)),
        "mask1": np.ascontiguousarray(mask_tile_np(1)),
    }
    per_core = []
    for c in range(NCORES):
        b = c // 4
        tok = core_token_index(c)
        meta = np.zeros((1, 16), dtype=np.uint32)
        for k, s in enumerate(slots_for_core(c)):
            zend = Z + (s + 1) * STRIP
            meta[0, k] = zend
            meta[0, 4 + k] = zend * V65
            meta[0, 8 + k] = zend - LHAT[k] * 128
            meta[0, 12 + k] = (zend - LHAT[k] * 128) * V65
        per_core.append({
            "xb": np.ascontiguousarray(x[b].T),
            "xin": np.ascontiguousarray(x[b][tok].T),
            "meta": meta, "tok": tok, "b": b,
        })
    return shared, per_core


_NC_CACHE = {}


def _get_nc(phases):
    key = tuple(phases)
    if key not in _NC_CACHE:
        _NC_CACHE[key] = Build(list(phases)).build()
    return _NC_CACHE[key]


LAUNCH_PLAN = [["l0attn", "l0ffn", "l1attn", "l1ffn"]]
if os.environ.get("KMODE", "fused") == "multi":
    LAUNCH_PLAN = [["l0attn"], ["l0ffn"], ["l1attn"], ["l1ffn"]]


def _in_names(nc):
    names = set()
    for alloc in nc.m.functions[0].allocations:
        if isinstance(alloc, mybir.MemoryLocationSet) and alloc.kind == "ExternalInput":
            n = alloc.memorylocations[0].name
            if n != "partition_id":
                names.add(n)
    return names


def kernel(**inputs):
    try:
        import jax
        jax.config.update("jax_compilation_cache_dir", "/tmp/jaxcache_kernel")
        jax.config.update("jax_persistent_cache_min_entry_size_bytes", 0)
        jax.config.update("jax_persistent_cache_min_compile_time_secs", 0)
    except Exception:
        pass
    shared, per_core = _prep(inputs)
    xcur = [pc["xin"] for pc in per_core]
    for phases in LAUNCH_PLAN:
        nc = _get_nc(phases)
        names = _in_names(nc)
        in_maps = []
        for c in range(NCORES):
            m = {}
            for n in names:
                if n == "xin":
                    m[n] = xcur[c]
                elif n in ("xb", "meta"):
                    m[n] = per_core[c][n]
                else:
                    m[n] = shared[n]
            in_maps.append(m)
        res = run_bass_kernel_spmd(nc, in_maps, core_ids=list(range(NCORES)))
        xcur = [np.ascontiguousarray(np.asarray(res.results[c]["out"]))
                for c in range(NCORES)]
    out = np.zeros((2, TB, DM), dtype=np.float32)
    for c in range(NCORES):
        out[per_core[c]["b"], per_core[c]["tok"]] = xcur[c].T
    return out


